# version 03
# baseline (speedup 1.0000x reference)
import sys
import numpy as np
import ml_dtypes

sys.path.insert(0, "/opt/trn_rl_repo")
import concourse.bass as bass
import concourse.mybir as mybir
from concourse.bass_utils import run_bass_kernel_spmd

F32 = mybir.dt.float32
BF16 = mybir.dt.bfloat16
U8 = mybir.dt.uint8
AF = mybir.ActivationFunctionType
ALU = mybir.AluOpType

BF = ml_dtypes.bfloat16



import os as _os, json as _json
def env_cfg():
    s = _os.environ.get("CFGJSON")
    if s:
        return _json.loads(s)
    return None

def full_cfg():
    return dict(IC=32, KVH=8, G=4, NBLK=32, NOWN=8, MC=32)

def tiny_cfg():
    return dict(IC=4, KVH=1, G=4, NBLK=8, NOWN=2, MC=4)

R = 4
HD = 128
TB = 128
GA1 = 4
EXP_SHIFT = -25.0

def derived(cfg):
    d = dict(cfg)
    d["D"] = cfg["IC"] * 128
    d["S"] = cfg["NBLK"] * TB
    d["TPC"] = cfg["NOWN"] * TB
    d["M"] = cfg["MC"] * 128
    d["QOC"] = cfg["KVH"] * cfg["G"]
    assert d["QOC"] == cfg["IC"]
    d["KVOC"] = 2 * cfg["KVH"]
    d["NG1"] = cfg["NBLK"] // GA1
    d["OWNG"] = cfg["NOWN"] // 2
    d["NA2T"] = d["OWNG"] * TB
    assert d["NA2T"] <= 512
    d["NSLAB_A1"] = d["NG1"] * d["KVOC"]
    d["NSLAB"] = (d["NSLAB_A1"] + d["QOC"] + cfg["IC"] + 2 * cfg["MC"] + cfg["IC"])
    pads = []
    for m in range(cfg["NOWN"] // 2):
        pads.append(2 * R * m + R)
        pads.append(2 * R * m + 2 * R)
    d["PADS"] = pads
    return d

def own_blocks(s, cfg):
    out = []
    for m in range(cfg["NOWN"] // 2):
        out += [2 * R * m + s, 2 * R * m + 2 * R - 1 - s]
    return sorted(out)


ENGS = ("tensor", "scalar", "vector", "gpsimd", "sync")

class Prog:
    def __init__(self):
        self.ops = {e: [] for e in ENGS}
        self.cnt = {}
        self.sems = {}

    def sem(self, name, handle):
        self.sems[name] = handle
        self.cnt[name] = 0

    def op(self, eng, fn, waits=(), inc=None, incn=1):
        w = tuple((s, v) for (s, v) in waits if v > 0)
        self.ops[eng].append((fn, w, inc, incn))
        if inc is not None:
            self.cnt[inc] += incn
        return self.cnt[inc] if inc is not None else None

    def v(self, sem):
        return self.cnt[sem]

    def replay(self, eng, obj):
        for fn, waits, inc, incn in self.ops[eng]:
            for s, v in waits:
                obj.wait_ge(self.sems[s], v)
            ins = fn(obj)
            if inc is not None:
                ins.then_inc(self.sems[inc], incn)

    def audit(self):
        for e in ENGS:
            for fn, waits, inc, incn in self.ops[e]:
                for s, v in waits:
                    assert v <= self.cnt[s], f"wait {s}>={v} unreachable (max {self.cnt[s]})"


def build(cfg):
    d = derived(cfg)
    IC, KVH, G, NBLK, NOWN, MC = (cfg[k] for k in ("IC", "KVH", "G", "NBLK", "NOWN", "MC"))
    D, S, TPC, M, QOC, KVOC = (d[k] for k in ("D", "S", "TPC", "M", "QOC", "KVOC"))
    NG1, OWNG, NA2T, PADS = d["NG1"], d["OWNG"], d["NA2T"], d["PADS"]
    NSLAB = d["NSLAB"]
    QSCALE = float(1.0 / np.sqrt(128.0))
    NTH = max(1, TPC // 512)
    THW = TPC // NTH
    NB_TH = THW // 128

    nc = bass.Bass("TRN2", target_bir_lowering=False)

    xbt = nc.dram_tensor("xb", [NBLK, TB, D], F32, kind="ExternalInput").ap()
    xow = nc.dram_tensor("xown", [NOWN, TB, D], F32, kind="ExternalInput").ap()
    wtape = nc.dram_tensor("wtape", [NSLAB, 128, IC, 128], F32, kind="ExternalInput").ap()
    mtab = nc.dram_tensor("mtab", [NOWN, R, 128, 512], BF16, kind="ExternalInput").ap()
    idon = nc.dram_tensor("idon", [2, 128, 128], BF16, kind="ExternalInput").ap()
    bcst = nc.dram_tensor("bcst", [128, 2 * MC + IC + 2], F32, kind="ExternalInput").ap()
    yout = nc.dram_tensor("yout", [NOWN, TB, D], F32, kind="ExternalOutput").ap()

    KDR = nc.dram_tensor("KDR", [KVH, 128, S], BF16).ap()
    VDR = nc.dram_tensor("VDR", [KVH, NBLK, TB, 128], BF16).ap()
    QDR = nc.dram_tensor("QDR", [KVH, G, 128, TPC], BF16).ap()
    X2DR = nc.dram_tensor("X2DR", [NOWN, TB, D], BF16).ap()

    p = Prog()
    from contextlib import ExitStack
    ctx = ExitStack()

    BASE0 = ((nc.sbuf_base + 31) // 32) * 32
    AR_BYTES = nc.sbuf_top - BASE0

    def sb(name, shape, dtype, off):
        per_part = int(np.prod(shape[1:])) * mybir.dt.size(dtype)
        assert off + per_part <= AR_BYTES, f"{name}: {off}+{per_part} > {AR_BYTES}"
        h = nc.alloc_sbuf_tensor_at(name, shape, dtype, offset=BASE0 + off)
        return h.ap()

    KB = 1024
    A1_OFF, A2_OFF, A3_OFF, MI_OFF = 0, 64 * KB, 128 * KB, 192 * KB
    xscr = sb("xscr", [128, 2, D], BF16, A1_OFF)
    xnscr = sb("xnscr", [128, 2, D], BF16, A1_OFF + 2 * D * 2)
    attn = sb("attn", [128, IC, TPC], BF16, A1_OFF)
    x4 = sb("x4", [128, MC, TPC], BF16, A1_OFF)
    xstrip = sb("xstrip", [128, 2, D], BF16, A1_OFF)
    x2stage = sb("x2stage", [128, 2, D], BF16, A1_OFF + 2 * D * 2)
    ostage = sb("ostage", [128, 2, D], F32, A1_OFF + 4 * D * 2)
    xnT2 = sb("xnT2", [128, 2, IC, 512], BF16, A2_OFF)
    ytm = sb("ytm", [128, NOWN, D], BF16, A2_OFF)
    kstage = sb("kstage", [128, 2, 512], BF16, A3_OFF)
    vstage = sb("vstage", [128, 2, 512], BF16, A3_OFF + 2 * KB)
    vtstage = sb("vtstage", [128, 2, 4, 128], BF16, A3_OFF + 4 * KB)
    qstage = sb("qstage", [128, 2, 512], BF16, A3_OFF + 6 * KB)
    ksl = sb("ksl", [128, 2, NBLK, 128], BF16, A3_OFF)
    vsl = sb("vsl", [128, 2, NBLK, 128], BF16, A3_OFF + 2 * NBLK * 128 * 2)
    _a3c = A3_OFF + 4 * NBLK * 128 * 2
    qsl = sb("qsl", [128, 2, G, 128], BF16, _a3c)
    psl = sb("psl", [128, 3, 512], BF16, _a3c + 2 * KB)
    lntmp = sb("lntmp", [128, 2, 512], F32, _a3c + 5 * KB)
    msl = sb("msl", [128, 2, R, 512], BF16, _a3c + 9 * KB)
    ydst = sb("ydst", [128, 2, 512], BF16, _a3c + 17 * KB)
    xn2T = sb("xn2T", [128, IC, TPC], BF16, A3_OFF)
    idsb = sb("idsb", [128, 128], BF16, MI_OFF)
    onesb = sb("onesb", [128, 128], BF16, MI_OFF + 256)
    NBC = 2 * MC + IC + 2
    bsb = sb("bsb", [128, NBC], F32, MI_OFF + 512)
    SHIFT_COL = 2 * MC + IC + 1
    _st_off = MI_OFF + 512 + ((NBC * 4 + 31) // 32) * 32
    stats = sb("stats", [128, 2, 48], F32, _st_off)
    mvt = sb("mvt", [128, 2, 2], F32, _st_off + 384)
    ltm = sb("ltm", [128, 2, 1], F32, _st_off + 448)
    rst = sb("rst", [128, 2, 1], F32, _st_off + 480)
    silst = sb("silst", [128, 2, 512], BF16, A2_OFF)
    IC2 = IC // 2
    wsl = sb("wsl", [128, 3, IC2, 128], BF16, _st_off + 512)
    EPS_COL = 2 * MC + IC

    pp = ctx.enter_context(nc.psum_tensor("pp", [128, 7, 512], F32))
    pt = ctx.enter_context(nc.psum_tensor("pt", [128, 8, 128], BF16))

    for s_ in ("dCst", "dX0", "dX1", "dW0", "dW1", "dW2",
               "dSK0", "dSK1", "dSV0", "dSV1", "dSQ0", "dSQ1", "dSX0", "dSX1",
               "dLK0", "dLK1", "dLV0", "dLV1", "dLQ0", "dLQ1", "dLM0", "dLM1",
               "dLX0", "dLX1", "dO0", "dO1",
               "mPE", "mQK", "mIT", "mTR",
               "aEV", "aEX", "aR", "vLN", "vEV", "vAT", "vMS"):
        p.sem(s_, ctx.enter_context(nc.semaphore(s_)))

    half_loaded = [0]
    half_done = {}

    def load_half():
        n = half_loaded[0]
        slot = n % 3
        waits = [half_done[n - 3]] if (n - 3) in half_done else []
        if n == 0:
            waits = [("dCst", 48)]
        sem = f"dW{slot}"
        p.op("gpsimd", lambda g, n=n, slot=slot: g.dma_start(
            wsl[:, slot, :, :], wtape[n // 2][:, (n % 2) * IC2:(n % 2 + 1) * IC2, :]),
            waits=waits, inc=sem, incn=16)
        half_loaded[0] += 1
        return (n, slot, (sem, p.v(sem)))

    def load_slab():
        return (load_half(), load_half())

    WAR = {}

    def ln_stats(in_ap, par, first_waits=()):
        nst = D // 512
        for k in range(nst):
            p.op("vector", lambda v, k=k, par=par, in_ap=in_ap: v.bn_stats(
                stats[:, par, 6 * k:6 * k + 6], in_ap[:, 512 * k:512 * (k + 1)]),
                waits=(list(first_waits) if k == 0 else []),
                inc="vLN" if k == nst - 1 else None)
        vs = p.v("vLN")
        p.op("vector", lambda v, par=par, nst=nst: v.bn_aggr(
            mvt[:, par, 0:2], stats[:, par, 0:6 * nst]),
            waits=[("vLN", vs), WAR.get(("mv", par), ("aR", 0))], inc="vLN")
        va = p.v("vLN")
        p.op("scalar", lambda s, par=par: s.activation(
            ltm[:, par, :], mvt[:, par, 1:2], AF.Ln, bias=bsb[:, EPS_COL:EPS_COL + 1]),
            waits=[("vLN", va)], inc="aR")
        ar1 = p.v("aR")
        p.op("scalar", lambda s, par=par: s.activation(
            rst[:, par, :], ltm[:, par, :], AF.Exp, scale=-0.5),
            waits=[("aR", ar1)], inc="aR")
        ar = p.v("aR")
        WAR[("mv", par)] = ("aR", ar)
        return ar

    def ln_norm(out_ap, in_ap, par, ar_val, extra=()):
        p.op("vector", lambda v, out_ap=out_ap, in_ap=in_ap, par=par: v.tensor_scalar(
            out_ap, in_ap, mvt[:, par, 0:1], rst[:, par, 0:1], ALU.subtract, ALU.mult),
            waits=[("aR", ar_val)] + list(extra), inc="vLN")
        return p.v("vLN")

    pt_free = [("vEV", 0)]

    def transpose_batch(tiles_fn, n, dst_ap, first_waits=(), dst_war=()):
        for j in range(n):
            p.op("tensor", lambda t, j=j, tiles_fn=tiles_fn: t.transpose(
                pt[:, j, :], tiles_fn(j), idsb),
                waits=(list(first_waits) + [pt_free[0]] if j == 0 else []),
                inc="mTR" if j == n - 1 else None)
        tr = p.v("mTR")
        p.op("vector", lambda v, dst_ap=dst_ap, n=n: v.tensor_copy(dst_ap, pt[:, 0:n, :]),
             waits=[("mTR", tr)] + list(dst_war), inc="vEV")
        pt_free[0] = ("vEV", p.v("vEV"))
        return p.v("vEV")

    def transpose_strip(src_ap, nchunks, dst_fn, src_ready, dst_war=()):
        last = None
        for ic0 in range(0, nchunks, 8):
            n = min(8, nchunks - ic0)
            last = transpose_batch(
                lambda j, ic0=ic0, src_ap=src_ap: src_ap[:, 128 * (ic0 + j):128 * (ic0 + j + 1)],
                n, dst_fn(ic0, n),
                first_waits=([src_ready] if ic0 == 0 else []),
                dst_war=(dst_war if ic0 == 0 else ()))
        return last

    bank_free = {i: ("aEV", 0) for i in range(7)}

    def proj_ochunk(bank, rhs_fn, halves, extra=(), nw=512):
        (n0, slot0, rdy0), (n1, slot1, rdy1) = halves
        for i in range(IC):
            hs = slot0 if i < IC2 else slot1
            if i == 0:
                waits = [rdy0, bank_free[bank]] + list(extra)
            elif i == IC2:
                waits = [rdy1]
            else:
                waits = []
            p.op("tensor", lambda t, i=i, bank=bank, hs=hs, rhs_fn=rhs_fn, nw=nw: t.matmul(
                pp[:, bank, 0:nw], wsl[:, hs, i % IC2, :], rhs_fn(i),
                start=(i == 0), stop=(i == IC - 1)),
                waits=waits,
                inc="mPE" if (i == IC2 - 1 or i == IC - 1) else None)
            if i == IC2 - 1:
                half_done[n0] = ("mPE", p.v("mPE"))
        half_done[n1] = ("mPE", p.v("mPE"))
        return p.v("mPE")

    def sdma(sem, dst_ap, src_ap, waits):
        p.op("sync", lambda sy, dst_ap=dst_ap, src_ap=src_ap: sy.dma_start(dst_ap, src_ap),
             waits=waits, inc=sem, incn=16)
        return (sem, p.v(sem))

    p.op("gpsimd", lambda g: g.dma_start(idsb, idon[0]), inc="dCst", incn=16)
    p.op("gpsimd", lambda g: g.dma_start(onesb, idon[1]), inc="dCst", incn=16)
    p.op("gpsimd", lambda g: g.dma_start(bsb, bcst), inc="dCst", incn=16)

    def prep_block(strip_src, par, xnT_slot, col0, is_first):
        war = [WAR.get(("xscr", par), ("vLN", 0))]
        if is_first:
            war.append(("dCst", 48))
        p.op("gpsimd", lambda g, strip_src=strip_src, par=par: g.dma_start(
            xscr[:, par, :], strip_src), waits=war, inc=f"dX{par}", incn=16)
        dx = (f"dX{par}", p.v(f"dX{par}"))
        ar = ln_stats(xscr[:, par, :], par, first_waits=[dx])
        nv = ln_norm(xnscr[:, par, :], xscr[:, par, :], par, ar,
                     extra=[WAR.get(("xnscr", par), ("mTR", 0))])
        WAR[("xscr", par)] = ("vLN", nv)
        ev = transpose_strip(
            xnscr[:, par, :], IC,
            lambda ic0, n, xnT_slot=xnT_slot, col0=col0: xnT2[:, xnT_slot, ic0:ic0 + n, col0:col0 + 128],
            src_ready=("vLN", nv),
            dst_war=[WAR.get(("xnT", xnT_slot), ("mPE", 0))])
        WAR[("xnscr", par)] = ("mTR", p.v("mTR"))
        return ev

    pb = 0
    for grp in range(NG1):
        evs = []
        for j in range(GA1):
            blk = grp * GA1 + j
            evs.append(prep_block(xbt[blk], pb % 2, grp % 2, j * 128,
                                  is_first=(grp == 0 and j == 0)))
            pb += 1
        xn_ok = ("vEV", evs[-1])
        for o in range(KVOC):
            halves = load_slab()
            bank = o % 4
            mm = proj_ochunk(bank, lambda i, g2=grp % 2: xnT2[:, g2, i, 0:512],
                             halves, extra=[xn_ok], nw=512)
            par = (grp * KVOC + o) % 2
            if o < KVH:
                p.op("scalar", lambda s, bank=bank, par=par: s.activation(
                    kstage[:, par, :], pp[:, bank, :], AF.Copy),
                    waits=[("mPE", mm), WAR.get(("kst", par), ("aR", 0))], inc="aEV")
                bank_free[bank] = ("aEV", p.v("aEV"))
                sv = sdma(f"dSK{par}", KDR[o][:, grp * 512:(grp + 1) * 512],
                          kstage[:, par, :], [("aEV", p.v("aEV"))])
                WAR[("kst", par)] = sv
            else:
                kv = o - KVH
                p.op("scalar", lambda s, bank=bank, par=par: s.activation(
                    vstage[:, par, :], pp[:, bank, :], AF.Copy),
                    waits=[("mPE", mm), WAR.get(("vst", par), ("mTR", 0))], inc="aEV")
                bank_free[bank] = ("aEV", p.v("aEV"))
                aev = p.v("aEV")
                ev = transpose_batch(
                    lambda j, par=par: vstage[:, par, 128 * j:128 * (j + 1)],
                    4, vtstage[:, par, :, :],
                    first_waits=[("aEV", aev)],
                    dst_war=[WAR.get(("vts", par), ("aR", 0))])
                WAR[("vst", par)] = ("mTR", p.v("mTR"))
                sv = sdma(f"dSV{par}",
                          VDR[kv, grp * GA1:(grp + 1) * GA1].rearrange("b t d -> t b d"),
                          vtstage[:, par, :, :], [("vEV", ev)])
                WAR[("vts", par)] = sv
        WAR[("xnT", grp % 2)] = ("mPE", p.v("mPE"))

    A1_K = [("dSK0", p.v("dSK0")), ("dSK1", p.v("dSK1"))]
    A1_V = [("dSV0", p.v("dSV0")), ("dSV1", p.v("dSV1"))]

    for gi in range(2):
        for j in range(OWNG):
            prep_block(xow[gi * OWNG + j], pb % 2, gi, j * 128, is_first=False)
            pb += 1
    a2_ok = ("vEV", p.v("vEV"))

    for o in range(QOC):
        halves = load_slab()
        for gi in range(2):
            bank = (2 * o + gi) % 4
            mm = proj_ochunk(bank, lambda i, gi=gi: xnT2[:, gi, i, 0:NA2T],
                             halves, extra=[a2_ok], nw=NA2T)
            par = (2 * o + gi) % 2
            p.op("scalar", lambda s, bank=bank, par=par: s.activation(
                qstage[:, par, 0:NA2T], pp[:, bank, 0:NA2T], AF.Copy, scale=QSCALE),
                waits=[("mPE", mm), WAR.get(("qst", par), ("aR", 0))], inc="aEV")
            bank_free[bank] = ("aEV", p.v("aEV"))
            sv = sdma(f"dSQ{par}", QDR[o // G, o % G, :, gi * NA2T:(gi + 1) * NA2T],
                      qstage[:, par, 0:NA2T], [("aEV", p.v("aEV"))])
            WAR[("qst", par)] = sv

    A2_Q = [("dSQ0", p.v("dSQ0")), ("dSQ1", p.v("dSQ1"))]
    A_PE_DONE = p.v("mPE")
    A_TR_DONE = p.v("mTR")

    nQK = 0
    exp_vals = []
    it_vals = []
    kv_last_it = {}
    qsl_war = {}
    msl_war = {}
    pv_free = {0: ("vAT", 0), 1: ("vAT", 0)}
    ones_free = {0: ("vAT", 0), 1: ("vAT", 0)}
    lnt_war = {0: ("vAT", 0), 1: ("vAT", 0)}
    QKB = (0, 1, 4)
    PVB = {0: 2, 1: 5}
    ONB = {0: 3, 1: 6}

    def emit_qk(i, t, slot, qslot, krdy, qrdy):
        sbank = QKB[i % 3]
        waits = []
        if t == 0:
            waits += [krdy, qrdy]
        if i >= 3:
            waits.append(("aEX", exp_vals[i - 3]))
        p.op("tensor", lambda t_, sbank=sbank, slot=slot, qslot=qslot, tt=t: t_.matmul(
            pp[:, sbank, :], ksl[:, slot, tt, :],
            qsl[:, qslot].rearrange("p g t -> p (g t)"),
            start=True, stop=True, skip_group_check=True),
            waits=waits, inc="mQK")
        ewaits = [("mQK", p.v("mQK"))]
        if i >= 3:
            ewaits.append(("mIT", it_vals[i - 3]))
        p.op("scalar", lambda s, sbank=sbank, sl3=i % 3: s.activation(
            psl[:, sl3, :], pp[:, sbank, :], AF.Exp,
            bias=bsb[:, SHIFT_COL:SHIFT_COL + 1]),
            waits=ewaits, inc="aEX")
        exp_vals.append(p.v("aEX"))

    for kv in range(KVH):
        slot = kv % 2
        kwaits = list(A1_K)
        vwaits = list(A1_V)
        if kv >= 2:
            kwaits.append(("mIT", kv_last_it[kv - 2]))
        krdy = sdma(f"dLK{slot}", ksl[:, slot, :, :],
                    KDR[kv].rearrange("p (c t) -> p c t", c=NBLK), kwaits)
        vrdy = sdma(f"dLV{slot}", vsl[:, slot, :, :],
                    VDR[kv].rearrange("b t d -> t b d"), vwaits)
        for qb in range(NOWN):
            pad = PADS[qb]
            par = qb % 2
            qslot = qb % 2
            qwaits = list(A2_Q)
            if qslot in qsl_war:
                qwaits.append(("mQK", qsl_war[qslot]))
            qrdy = sdma(f"dLQ{qslot}", qsl[:, qslot, :, :],
                        QDR[kv, :, :, qb * 128:(qb + 1) * 128].rearrange("g d t -> d g t"),
                        qwaits)
            mwaits = [("vMS", msl_war[qslot])] if qslot in msl_war else []
            mrdy = sdma(f"dLM{qslot}", msl[:, qslot, :, :],
                        mtab[qb].rearrange("w p q -> p w q"), mwaits)

            base = nQK
            for t in range(min(3, pad)):
                emit_qk(base + t, t, slot, qslot, krdy, qrdy)
            for t in range(pad):
                i = base + t
                sl3 = i % 3
                pv_gate = ("aEX", exp_vals[i])
                if t >= pad - R:
                    w = t - (pad - R)
                    p.op("vector", lambda v, sl3=sl3, qslot=qslot, w=w: v.tensor_tensor(
                        psl[:, sl3, :], psl[:, sl3, :], msl[:, qslot, w, :], ALU.mult),
                        waits=[("aEX", exp_vals[i]), mrdy], inc="vMS")
                    pv_gate = ("vMS", p.v("vMS"))
                    msl_war[qslot] = p.v("vMS")
                pvw = [pv_gate]
                if t == 0:
                    pvw += [vrdy, pv_free[par]]
                p.op("tensor", lambda t_, slot=slot, sl3=sl3, tt=t, pb=PVB[par],
                     st=(t == 0), sp=(t == pad - 1): t_.matmul(
                    pp[:, pb, :], vsl[:, slot, tt, :], psl[:, sl3, :],
                    start=st, stop=sp, skip_group_check=True),
                    waits=pvw)
                ow = [ones_free[par]] if t == 0 else []
                p.op("tensor", lambda t_, sl3=sl3, ob=ONB[par],
                     st=(t == 0), sp=(t == pad - 1): t_.matmul(
                    pp[:, ob, :], onesb, psl[:, sl3, :],
                    start=st, stop=sp, skip_group_check=True),
                    waits=ow, inc="mIT")
                nQK += 1
                it_vals.append(p.v("mIT"))
                if t + 3 < pad:
                    emit_qk(base + t + 3, t + 3, slot, qslot, krdy, qrdy)
            qsl_war[qslot] = p.v("mQK")
            itv = p.v("mIT")
            p.op("vector", lambda v, par=par: v.reciprocal(lntmp[:, par, :], pp[:, ONB[par], :]),
                 waits=[("mIT", itv), lnt_war[par]], inc="vAT")
            rcv = p.v("vAT")
            ones_free[par] = ("vAT", rcv)
            awar = [("mTR", A_TR_DONE), ("vLN", p.v("vLN"))] if (kv, qb) == (0, 0) else []
            p.op("vector", lambda v, kv=kv, qb=qb, par=par: v.tensor_tensor(
                attn[:, G * kv:G * (kv + 1), qb * 128:(qb + 1) * 128],
                pp[:, PVB[par], :].rearrange("p (g t) -> p g t", g=G),
                lntmp[:, par, :].rearrange("p (g t) -> p g t", g=G), ALU.mult),
                waits=[("vAT", rcv)] + awar, inc="vAT")
            pv_free[par] = ("vAT", p.v("vAT"))
            lnt_war[par] = ("vAT", p.v("vAT"))
        kv_last_it[kv] = p.v("mIT")

    B_AT = p.v("vAT")
    B_IT = p.v("mIT")
    B_MS = p.v("vMS")
    B_EX = p.v("aEX")
    bank_free[0] = ("aEX", B_EX)
    bank_free[1] = ("aEX", B_EX)
    bank_free[4] = ("aEX", B_EX)
    bank_free[2] = ("vAT", B_AT)
    bank_free[3] = ("vAT", B_AT)
    bank_free[5] = ("vAT", B_AT)
    bank_free[6] = ("vAT", B_AT)

    for o in range(IC):
        halves = load_slab()
        for th in range(NTH):
            bank = (NTH * o + th) % 4
            extra = [("vAT", B_AT)] if (o == 0 and th == 0) else []
            mm = proj_ochunk(bank, lambda i, th=th: attn[:, i, th * THW:(th + 1) * THW],
                             halves, extra=extra, nw=THW)
            par = th % 2
            p.op("scalar", lambda s, bank=bank, par=par: s.activation(
                ydst[:, par, 0:THW], pp[:, bank, 0:THW], AF.Copy),
                waits=[("mPE", mm), WAR.get(("yd", par), ("mTR", 0))], inc="aEV")
            bank_free[bank] = ("aEV", p.v("aEV"))
            aev = p.v("aEV")
            ev = transpose_batch(
                lambda j, par=par: ydst[:, par, 128 * j:128 * (j + 1)],
                NB_TH, ytm[:, th * NB_TH:(th + 1) * NB_TH, o * 128:(o + 1) * 128],
                first_waits=[("aEV", aev)],
                dst_war=([("mPE", A_PE_DONE)] if (o == 0 and th == 0) else ()))
            WAR[("yd", par)] = ("mTR", p.v("mTR"))
    C_MM = p.v("mPE")
    C_EV = p.v("vEV")

    for tb in range(NOWN):
        par = tb % 2
        w0 = [("mPE", C_MM)] if tb < 2 else [WAR.get(("xstr", par), ("vLN", 0))]
        p.op("gpsimd", lambda g, tb=tb, par=par: g.dma_start(xstrip[:, par, :], xow[tb]),
             waits=w0, inc=f"dX{par}", incn=16)
        dx = (f"dX{par}", p.v(f"dX{par}"))
        ar = ln_stats(ytm[:, tb, :], par, first_waits=[("vEV", C_EV)])
        x2w = WAR.get(("x2s", par), ("aR", 0))
        p.op("vector", lambda v, tb=tb, par=par: v.tensor_scalar(
            x2stage[:, par, :], ytm[:, tb, :], mvt[:, par, 0:1], rst[:, par, 0:1],
            ALU.subtract, ALU.mult),
            waits=[("aR", ar), x2w] + ([("mTR", A_TR_DONE)] if tb < 2 else []), inc="vLN")
        vv = p.v("vLN")
        p.op("vector", lambda v, par=par: v.tensor_tensor(
            x2stage[:, par, :], x2stage[:, par, :], xstrip[:, par, :], ALU.add),
            waits=[("vLN", vv), dx], inc="vLN")
        vv2 = p.v("vLN")
        sv = sdma(f"dSX{par}", X2DR[tb], x2stage[:, par, :], [("vLN", vv2)])
        WAR[("x2s", par)] = sv
        ar3 = ln_stats(x2stage[:, par, :], par, first_waits=[("vLN", vv2)])
        nv = ln_norm(xstrip[:, par, :], x2stage[:, par, :], par, ar3)
        dst_war = ([("mIT", B_IT), ("vAT", B_AT), ("vMS", B_MS)] if tb == 0 else ())
        transpose_strip(
            xstrip[:, par, :], IC,
            lambda ic0, n, tb=tb: xn2T[:, ic0:ic0 + n, tb * 128:(tb + 1) * 128],
            src_ready=("vLN", nv), dst_war=dst_war)
        WAR[("xstr", par)] = ("mTR", p.v("mTR"))
    XN2T_OK = ("vEV", p.v("vEV"))
    X2_ST = [("dSX0", p.v("dSX0")), ("dSX1", p.v("dSX1"))]

    for m in range(MC):
        h1 = load_slab()
        mm1 = {}
        mm2 = {}
        for th in range(NTH):
            extra = [XN2T_OK] if (m == 0 and th == 0) else []
            mm1[th] = proj_ochunk(th % 2, lambda i, th=th: xn2T[:, i, th * THW:(th + 1) * THW],
                                  h1, extra=extra, nw=THW)
        h2 = load_slab()
        for th in range(NTH):
            mm2[th] = proj_ochunk(2 + th % 2, lambda i, th=th: xn2T[:, i, th * THW:(th + 1) * THW],
                                  h2, nw=THW)
        for th in range(NTH):
            par = th % 2
            p.op("scalar", lambda s, th=th, par=par, m=m: s.activation(
                silst[:, par, 0:THW], pp[:, th % 2, 0:THW], AF.Silu, bias=bsb[:, m:m + 1]),
                waits=[("mPE", mm1[th]), WAR.get(("sil", par), ("vLN", 0))], inc="aEV")
            bank_free[th % 2] = ("aEV", p.v("aEV"))
            aev = p.v("aEV")
            x4w = ([("mPE", C_MM), ("mTR", p.v("mTR"))] + X2_ST) if (m == 0 and th == 0) else []
            p.op("vector", lambda v, th=th, par=par, m=m: v.scalar_tensor_tensor(
                x4[:, m, th * THW:(th + 1) * THW], pp[:, 2 + th % 2, 0:THW], bsb[:, MC + m:MC + m + 1],
                silst[:, par, 0:THW], ALU.add, ALU.mult),
                waits=[("mPE", mm2[th]), ("aEV", aev)] + x4w, inc="vLN")
            bank_free[2 + th % 2] = ("vLN", p.v("vLN"))
            WAR[("sil", par)] = ("vLN", p.v("vLN"))
    X4_OK = ("vLN", p.v("vLN"))

    for o in range(IC):
        halves = load_slab()
        for th in range(NTH):
            bank = 4 + th % 2
            extra = [X4_OK] if (o == 0 and th == 0) else []
            mm = proj_ochunk(bank, lambda i, th=th: x4[:, i, th * THW:(th + 1) * THW],
                             halves, extra=extra, nw=THW)
            par = th % 2
            p.op("vector", lambda v, bank=bank, par=par, o=o: v.tensor_scalar_add(
                ydst[:, par, 0:THW], pp[:, bank, 0:THW],
                bsb[:, 2 * MC + o:2 * MC + o + 1]),
                waits=[("mPE", mm), WAR.get(("yd", par), ("mTR", 0))], inc="vEV")
            bank_free[bank] = ("vEV", p.v("vEV"))
            aev = p.v("vEV")
            transpose_batch(
                lambda j, par=par: ydst[:, par, 128 * j:128 * (j + 1)],
                NB_TH, ytm[:, th * NB_TH:(th + 1) * NB_TH, o * 128:(o + 1) * 128],
                first_waits=[("vEV", aev)])
            WAR[("yd", par)] = ("mTR", p.v("mTR"))
    E_MM = p.v("mPE")
    E_EV = p.v("vEV")

    for tb in range(NOWN):
        par = tb % 2
        w0 = ([("mPE", E_MM)] + X2_ST) if tb < 2 else [WAR.get(("xse", par), ("vLN", 0))]
        xrdy = sdma(f"dLX{par}", xstrip[:, par, :], X2DR[tb], w0)
        ar = ln_stats(ytm[:, tb, :], par, first_waits=[("vEV", E_EV)])
        p.op("vector", lambda v, tb=tb, par=par: v.tensor_scalar(
            x2stage[:, par, :], ytm[:, tb, :], mvt[:, par, 0:1], rst[:, par, 0:1],
            ALU.subtract, ALU.mult),
            waits=[("aR", ar), WAR.get(("x2e", par), ("vLN", 0))], inc="vLN")
        vv = p.v("vLN")
        p.op("vector", lambda v, par=par: v.tensor_tensor(
            ostage[:, par, :], x2stage[:, par, :], xstrip[:, par, :], ALU.add),
            waits=[("vLN", vv), xrdy, WAR.get(("ost", par), ("aR", 0))],
            inc="vLN")
        vv2 = p.v("vLN")
        WAR[("x2e", par)] = ("vLN", vv2)
        WAR[("xse", par)] = ("vLN", vv2)
        p.op("sync", lambda sy, tb=tb, par=par: sy.dma_start(yout[tb], ostage[:, par, :]),
             waits=[("vLN", vv2)], inc=f"dO{par}", incn=16)
        WAR[("ost", par)] = (f"dO{par}", p.v(f"dO{par}"))

    p.op("sync", lambda sy: sy.nop(),
         waits=[("dO0", p.v("dO0")), ("dO1", p.v("dO1"))])

    assert half_loaded[0] == 2 * NSLAB, (half_loaded[0], NSLAB)
    p.audit()

    with nc.Block() as block:
        @block.tensor
        def _(t):
            p.replay("tensor", t)

        @block.scalar
        def _(s):
            p.replay("scalar", s)

        @block.vector
        def _(v):
            p.replay("vector", v)

        @block.gpsimd
        def _(g):
            p.replay("gpsimd", g)

        @block.sync
        def _(sy):
            p.replay("sync", sy)

    ctx.close()
    return nc



def make_tape(cfg, w_qkv, w_dense, w12, w3):
    d = derived(cfg)
    IC, KVH, MC = cfg["IC"], cfg["KVH"], cfg["MC"]
    D, M = d["D"], d["M"]

    def slabs(W):
        O = W.shape[0]
        W4 = W.reshape(O // 128, 128, IC, 128)
        return np.ascontiguousarray(W4.transpose(0, 3, 2, 1))

    wq = w_qkv[0:D]
    wk = w_qkv[D:D + KVH * 128]
    wv = w_qkv[D + KVH * 128:D + 2 * KVH * 128]
    kv_slabs = np.concatenate([slabs(wk), slabs(wv)], axis=0)
    s1 = slabs(w12[0:M])
    s2 = slabs(w12[M:2 * M])
    inter = np.empty((2 * MC, 128, IC, 128), dtype=w12.dtype)
    inter[0::2] = s1
    inter[1::2] = s2
    tape = np.concatenate([np.concatenate([kv_slabs] * d["NG1"], axis=0),
                           slabs(wq), slabs(w_dense), inter, slabs(w3)], axis=0)
    assert tape.shape[0] == d["NSLAB"]
    return np.ascontiguousarray(tape)


def make_mtab(cfg, s):
    d = derived(cfg)
    NOWN = cfg["NOWN"]
    blocks = own_blocks(s, cfg)
    mt = np.zeros((NOWN, R, 128, 512), dtype=BF)
    tri = np.tril(np.ones((128, 128), dtype=np.float32))
    keep = np.tile(tri.T, (1, 4)).astype(BF)
    one = np.ones((128, 512), dtype=BF)
    for qb in range(NOWN):
        pad = d["PADS"][qb]
        jb = blocks[qb]
        for w in range(R):
            t_iter = pad - R + w
            if t_iter < jb:
                mt[qb, w] = one
            elif t_iter == jb:
                mt[qb, w] = keep
    return mt


def stage_inputs(cfg, x, w_qkv, w_dense, w12, b12, w3, b3):
    d = derived(cfg)
    NBLK, IC, MC = cfg["NBLK"], cfg["IC"], cfg["MC"]
    D, M = d["D"], d["M"]
    tape = make_tape(cfg, w_qkv, w_dense, w12, w3)
    idon = np.zeros((2, 128, 128), dtype=BF)
    idon[0] = np.eye(128, dtype=np.float32).astype(BF)
    idon[1] = 1.0
    bcst = np.zeros((128, 2 * MC + IC + 2), dtype=np.float32)
    bcst[:, 0:MC] = b12[0:M].reshape(MC, 128).T
    bcst[:, MC:2 * MC] = b12[M:2 * M].reshape(MC, 128).T
    bcst[:, 2 * MC:2 * MC + IC] = b3.reshape(IC, 128).T
    bcst[:, 2 * MC + IC] = 1e-5
    bcst[:, 2 * MC + IC + 1] = EXP_SHIFT
    in_maps = []
    for c in range(8):
        b, s = c // 4, c % 4
        blocks = own_blocks(s, cfg)
        xb_ = np.ascontiguousarray(x[b].reshape(NBLK, TB, D))
        xo_ = np.ascontiguousarray(xb_[blocks])
        in_maps.append(dict(xb=xb_, xown=xo_, wtape=tape,
                            mtab=make_mtab(cfg, s), idon=idon, bcst=bcst))
    return in_maps


def assemble(cfg, results):
    d = derived(cfg)
    out = np.zeros((2, d["S"], d["D"]), dtype=np.float32)
    for c in range(8):
        b, s = c // 4, c % 4
        blocks = own_blocks(s, cfg)
        y = np.asarray(results[c]["yout"])
        for i, blk in enumerate(blocks):
            out[b, blk * TB:(blk + 1) * TB] = y[i]
    return out


def kernel(L0_attn_q, L0_attn_w_qkv, L0_attn_w_dense, L0_ffn_w12, L0_ffn_b12,
           L0_ffn_w3, L0_ffn_b3):
    cfg = full_cfg()
    x = np.ascontiguousarray(np.asarray(L0_attn_q))
    in_maps = stage_inputs(cfg, x, np.asarray(L0_attn_w_qkv), np.asarray(L0_attn_w_dense),
                           np.asarray(L0_ffn_w12), np.asarray(L0_ffn_b12),
                           np.asarray(L0_ffn_w3), np.asarray(L0_ffn_b3))
    nc = build(cfg)
    res = run_bass_kernel_spmd(nc, in_maps, core_ids=list(range(8)))
    return assemble(cfg, res.results)


if __name__ == "__main__":
    cfg = tiny_cfg() if (len(sys.argv) > 1 and sys.argv[1] == "tiny") else full_cfg()
    nc = build(cfg)
    print("built ok")


# version 04
# speedup vs baseline: 1.0165x; 1.0165x over previous
import sys
import numpy as np
import ml_dtypes

sys.path.insert(0, "/opt/trn_rl_repo")
import concourse.bass as bass
import concourse.mybir as mybir
from concourse.bass_utils import run_bass_kernel_spmd

F32 = mybir.dt.float32
BF16 = mybir.dt.bfloat16
U8 = mybir.dt.uint8
AF = mybir.ActivationFunctionType
ALU = mybir.AluOpType

BF = ml_dtypes.bfloat16



import os as _os, json as _json
def env_cfg():
    s = _os.environ.get("CFGJSON")
    if s:
        return _json.loads(s)
    return None

def full_cfg():
    return dict(IC=32, KVH=8, G=4, NBLK=32, NOWN=8, MC=32)

def tiny_cfg():
    return dict(IC=4, KVH=1, G=4, NBLK=8, NOWN=2, MC=4)

R = 4
HD = 128
TB = 128
GA1 = 4
EXP_SHIFT = -25.0

def derived(cfg):
    d = dict(cfg)
    d["D"] = cfg["IC"] * 128
    d["S"] = cfg["NBLK"] * TB
    d["TPC"] = cfg["NOWN"] * TB
    d["M"] = cfg["MC"] * 128
    d["QOC"] = cfg["KVH"] * cfg["G"]
    assert d["QOC"] == cfg["IC"]
    d["KVOC"] = 2 * cfg["KVH"]
    d["NG1"] = cfg["NBLK"] // GA1
    d["OWNG"] = cfg["NOWN"] // 2
    d["NA2T"] = d["OWNG"] * TB
    assert d["NA2T"] <= 512
    d["NSLAB_A1"] = (d["NG1"] // 2) * d["KVOC"]
    d["NSLAB"] = (d["NSLAB_A1"] + d["QOC"] + cfg["IC"] + 2 * cfg["MC"] + cfg["IC"])
    pads = []
    for m in range(cfg["NOWN"] // 2):
        pads.append(2 * R * m + R)
        pads.append(2 * R * m + 2 * R)
    d["PADS"] = pads
    return d

def own_blocks(s, cfg):
    out = []
    for m in range(cfg["NOWN"] // 2):
        out += [2 * R * m + s, 2 * R * m + 2 * R - 1 - s]
    return sorted(out)


ENGS = ("tensor", "scalar", "vector", "gpsimd", "sync")

class Prog:
    def __init__(self):
        self.ops = {e: [] for e in ENGS}
        self.cnt = {}
        self.sems = {}

    def sem(self, name, handle):
        self.sems[name] = handle
        self.cnt[name] = 0

    def op(self, eng, fn, waits=(), inc=None, incn=1):
        w = tuple((s, v) for (s, v) in waits if v > 0)
        self.ops[eng].append((fn, w, inc, incn))
        if inc is not None:
            self.cnt[inc] += incn
        return self.cnt[inc] if inc is not None else None

    def v(self, sem):
        return self.cnt[sem]

    def replay(self, eng, obj):
        for fn, waits, inc, incn in self.ops[eng]:
            for s, v in waits:
                obj.wait_ge(self.sems[s], v)
            ins = fn(obj)
            if inc is not None:
                ins.then_inc(self.sems[inc], incn)

    def audit(self):
        for e in ENGS:
            for fn, waits, inc, incn in self.ops[e]:
                for s, v in waits:
                    assert v <= self.cnt[s], f"wait {s}>={v} unreachable (max {self.cnt[s]})"


def build(cfg):
    d = derived(cfg)
    IC, KVH, G, NBLK, NOWN, MC = (cfg[k] for k in ("IC", "KVH", "G", "NBLK", "NOWN", "MC"))
    D, S, TPC, M, QOC, KVOC = (d[k] for k in ("D", "S", "TPC", "M", "QOC", "KVOC"))
    NG1, OWNG, NA2T, PADS = d["NG1"], d["OWNG"], d["NA2T"], d["PADS"]
    NSLAB = d["NSLAB"]
    QSCALE = float(1.0 / np.sqrt(128.0))
    NTH = max(1, TPC // 512)
    THW = TPC // NTH
    NB_TH = THW // 128

    nc = bass.Bass("TRN2", target_bir_lowering=False)

    xbt = nc.dram_tensor("xb", [NBLK, TB, D], F32, kind="ExternalInput").ap()
    xow = nc.dram_tensor("xown", [NOWN, TB, D], F32, kind="ExternalInput").ap()
    wtape = nc.dram_tensor("wtape", [NSLAB, 128, IC, 128], F32, kind="ExternalInput").ap()
    mtab = nc.dram_tensor("mtab", [NOWN, R, 128, 512], BF16, kind="ExternalInput").ap()
    idon = nc.dram_tensor("idon", [2, 128, 128], BF16, kind="ExternalInput").ap()
    bcst = nc.dram_tensor("bcst", [128, 2 * MC + IC + 2], F32, kind="ExternalInput").ap()
    yout = nc.dram_tensor("yout", [NOWN, TB, D], F32, kind="ExternalOutput").ap()

    KDR = nc.dram_tensor("KDR", [KVH, 128, S], BF16).ap()
    VDR = nc.dram_tensor("VDR", [KVH, NBLK, TB, 128], BF16).ap()
    QDR = nc.dram_tensor("QDR", [KVH, G, 128, TPC], BF16).ap()
    X2DR = nc.dram_tensor("X2DR", [NOWN, TB, D], BF16).ap()

    p = Prog()
    from contextlib import ExitStack
    ctx = ExitStack()

    BASE0 = ((nc.sbuf_base + 31) // 32) * 32
    AR_BYTES = nc.sbuf_top - BASE0

    def sb(name, shape, dtype, off):
        per_part = int(np.prod(shape[1:])) * mybir.dt.size(dtype)
        assert off + per_part <= AR_BYTES, f"{name}: {off}+{per_part} > {AR_BYTES}"
        h = nc.alloc_sbuf_tensor_at(name, shape, dtype, offset=BASE0 + off)
        return h.ap()

    KB = 1024
    A1_OFF, A2_OFF, A3_OFF, MI_OFF = 0, 64 * KB, 128 * KB, 192 * KB
    xscr = sb("xscr", [128, 2, D], BF16, A1_OFF)
    xnscr = sb("xnscr", [128, 2, D], BF16, A1_OFF + 2 * D * 2)
    attn = sb("attn", [128, IC, TPC], BF16, A1_OFF)
    x4 = sb("x4", [128, MC, TPC], BF16, A1_OFF)
    xstrip = sb("xstrip", [128, 2, D], BF16, A1_OFF)
    x2stage = sb("x2stage", [128, 2, D], BF16, A1_OFF + 2 * D * 2)
    ostage = sb("ostage", [128, 2, D], F32, A1_OFF + 4 * D * 2)
    xnT2 = sb("xnT2", [128, 2, IC, 512], BF16, A2_OFF)
    ytm = sb("ytm", [128, NOWN, D], BF16, A2_OFF)
    kstage = sb("kstage", [128, 2, 512], BF16, A3_OFF)
    vstage = sb("vstage", [128, 2, 512], BF16, A3_OFF + 2 * KB)
    vtstage = sb("vtstage", [128, 2, 4, 128], BF16, A3_OFF + 4 * KB)
    qstage = sb("qstage", [128, 2, 512], BF16, A3_OFF + 6 * KB)
    ksl = sb("ksl", [128, 2, NBLK, 128], BF16, A3_OFF)
    vsl = sb("vsl", [128, 2, NBLK, 128], BF16, A3_OFF + 2 * NBLK * 128 * 2)
    _a3c = A3_OFF + 4 * NBLK * 128 * 2
    qsl = sb("qsl", [128, 2, G, 128], BF16, _a3c)
    psl = sb("psl", [128, 3, 512], BF16, _a3c + 2 * KB)
    lntmp = sb("lntmp", [128, 2, 512], F32, _a3c + 5 * KB)
    msl = sb("msl", [128, 2, R, 512], BF16, _a3c + 9 * KB)
    ydst = sb("ydst", [128, 2, 512], BF16, _a3c + 17 * KB)
    xn2T = sb("xn2T", [128, IC, TPC], BF16, A3_OFF)
    idsb = sb("idsb", [128, 128], BF16, MI_OFF)
    onesb = sb("onesb", [128, 128], BF16, MI_OFF + 256)
    NBC = 2 * MC + IC + 2
    bsb = sb("bsb", [128, NBC], F32, MI_OFF + 512)
    SHIFT_COL = 2 * MC + IC + 1
    _st_off = MI_OFF + 512 + ((NBC * 4 + 31) // 32) * 32
    stats = sb("stats", [128, 2, 48], F32, _st_off)
    mvt = sb("mvt", [128, 2, 2], F32, _st_off + 384)
    ltm = sb("ltm", [128, 2, 1], F32, _st_off + 448)
    rst = sb("rst", [128, 2, 1], F32, _st_off + 480)
    silst = sb("silst", [128, 2, 512], BF16, A2_OFF)
    IC2 = IC // 2
    wsl = sb("wsl", [128, 3, IC2, 128], BF16, _st_off + 512)
    EPS_COL = 2 * MC + IC

    pp = ctx.enter_context(nc.psum_tensor("pp", [128, 7, 512], F32))
    pt = ctx.enter_context(nc.psum_tensor("pt", [128, 8, 128], BF16))

    for s_ in ("dCst", "dX0", "dX1", "dW0", "dW1", "dW2",
               "dSK0", "dSK1", "dSV0", "dSV1", "dSQ0", "dSQ1", "dSX0", "dSX1",
               "dLK0", "dLK1", "dLV0", "dLV1", "dLQ0", "dLQ1", "dLM0", "dLM1",
               "dLX0", "dLX1", "dO0", "dO1",
               "mPE", "mQK", "mIT", "mTR",
               "aEV", "aEX", "aR", "vLN", "vEV", "vAT", "vMS"):
        p.sem(s_, ctx.enter_context(nc.semaphore(s_)))

    half_loaded = [0]
    half_done = {}

    def load_half():
        n = half_loaded[0]
        slot = n % 3
        waits = [half_done[n - 3]] if (n - 3) in half_done else []
        if n == 0:
            waits = [("dCst", 48)]
        sem = f"dW{slot}"
        p.op("gpsimd", lambda g, n=n, slot=slot: g.dma_start(
            wsl[:, slot, :, :], wtape[n // 2][:, (n % 2) * IC2:(n % 2 + 1) * IC2, :]),
            waits=waits, inc=sem, incn=16)
        half_loaded[0] += 1
        return (n, slot, (sem, p.v(sem)))

    def load_slab():
        return (load_half(), load_half())

    WAR = {}

    def ln_stats(in_ap, par, first_waits=()):
        nst = D // 512
        for k in range(nst):
            p.op("vector", lambda v, k=k, par=par, in_ap=in_ap: v.bn_stats(
                stats[:, par, 6 * k:6 * k + 6], in_ap[:, 512 * k:512 * (k + 1)]),
                waits=(list(first_waits) if k == 0 else []),
                inc="vLN" if k == nst - 1 else None)
        vs = p.v("vLN")
        p.op("vector", lambda v, par=par, nst=nst: v.bn_aggr(
            mvt[:, par, 0:2], stats[:, par, 0:6 * nst]),
            waits=[("vLN", vs), WAR.get(("mv", par), ("aR", 0))], inc="vLN")
        va = p.v("vLN")
        p.op("scalar", lambda s, par=par: s.activation(
            ltm[:, par, :], mvt[:, par, 1:2], AF.Ln, bias=bsb[:, EPS_COL:EPS_COL + 1]),
            waits=[("vLN", va)], inc="aR")
        ar1 = p.v("aR")
        p.op("scalar", lambda s, par=par: s.activation(
            rst[:, par, :], ltm[:, par, :], AF.Exp, scale=-0.5),
            waits=[("aR", ar1)], inc="aR")
        ar = p.v("aR")
        WAR[("mv", par)] = ("aR", ar)
        return ar

    def ln_norm(out_ap, in_ap, par, ar_val, extra=()):
        p.op("vector", lambda v, out_ap=out_ap, in_ap=in_ap, par=par: v.tensor_scalar(
            out_ap, in_ap, mvt[:, par, 0:1], rst[:, par, 0:1], ALU.subtract, ALU.mult),
            waits=[("aR", ar_val)] + list(extra), inc="vLN")
        return p.v("vLN")

    pt_free = [("vEV", 0)]

    def transpose_batch(tiles_fn, n, dst_ap, first_waits=(), dst_war=()):
        for j in range(n):
            p.op("tensor", lambda t, j=j, tiles_fn=tiles_fn: t.transpose(
                pt[:, j, :], tiles_fn(j), idsb),
                waits=(list(first_waits) + [pt_free[0]] if j == 0 else []),
                inc="mTR" if j == n - 1 else None)
        tr = p.v("mTR")
        p.op("vector", lambda v, dst_ap=dst_ap, n=n: v.tensor_copy(dst_ap, pt[:, 0:n, :]),
             waits=[("mTR", tr)] + list(dst_war), inc="vEV")
        pt_free[0] = ("vEV", p.v("vEV"))
        return p.v("vEV")

    def transpose_strip(src_ap, nchunks, dst_fn, src_ready, dst_war=()):
        last = None
        for ic0 in range(0, nchunks, 8):
            n = min(8, nchunks - ic0)
            last = transpose_batch(
                lambda j, ic0=ic0, src_ap=src_ap: src_ap[:, 128 * (ic0 + j):128 * (ic0 + j + 1)],
                n, dst_fn(ic0, n),
                first_waits=([src_ready] if ic0 == 0 else []),
                dst_war=(dst_war if ic0 == 0 else ()))
        return last

    bank_free = {i: ("aEV", 0) for i in range(7)}

    def proj_ochunk(bank, rhs_fn, halves, extra=(), nw=512):
        (n0, slot0, rdy0), (n1, slot1, rdy1) = halves
        for i in range(IC):
            hs = slot0 if i < IC2 else slot1
            if i == 0:
                waits = [rdy0, bank_free[bank]] + list(extra)
            elif i == IC2:
                waits = [rdy1]
            else:
                waits = []
            p.op("tensor", lambda t, i=i, bank=bank, hs=hs, rhs_fn=rhs_fn, nw=nw: t.matmul(
                pp[:, bank, 0:nw], wsl[:, hs, i % IC2, :], rhs_fn(i),
                start=(i == 0), stop=(i == IC - 1)),
                waits=waits,
                inc="mPE" if (i == IC2 - 1 or i == IC - 1) else None)
            if i == IC2 - 1:
                half_done[n0] = ("mPE", p.v("mPE"))
        half_done[n1] = ("mPE", p.v("mPE"))
        return p.v("mPE")

    def sdma(sem, dst_ap, src_ap, waits):
        p.op("sync", lambda sy, dst_ap=dst_ap, src_ap=src_ap: sy.dma_start(dst_ap, src_ap),
             waits=waits, inc=sem, incn=16)
        return (sem, p.v(sem))

    p.op("gpsimd", lambda g: g.dma_start(idsb, idon[0]), inc="dCst", incn=16)
    p.op("gpsimd", lambda g: g.dma_start(onesb, idon[1]), inc="dCst", incn=16)
    p.op("gpsimd", lambda g: g.dma_start(bsb, bcst), inc="dCst", incn=16)

    def prep_block(strip_src, par, xnT_slot, col0, is_first):
        war = [WAR.get(("xscr", par), ("vLN", 0))]
        if is_first:
            war.append(("dCst", 48))
        p.op("gpsimd", lambda g, strip_src=strip_src, par=par: g.dma_start(
            xscr[:, par, :], strip_src), waits=war, inc=f"dX{par}", incn=16)
        dx = (f"dX{par}", p.v(f"dX{par}"))
        ar = ln_stats(xscr[:, par, :], par, first_waits=[dx])
        nv = ln_norm(xnscr[:, par, :], xscr[:, par, :], par, ar,
                     extra=[WAR.get(("xnscr", par), ("mTR", 0))])
        WAR[("xscr", par)] = ("vLN", nv)
        ev = transpose_strip(
            xnscr[:, par, :], IC,
            lambda ic0, n, xnT_slot=xnT_slot, col0=col0: xnT2[:, xnT_slot, ic0:ic0 + n, col0:col0 + 128],
            src_ready=("vLN", nv),
            dst_war=[WAR.get(("xnT", xnT_slot), ("mPE", 0))])
        WAR[("xnscr", par)] = ("mTR", p.v("mTR"))
        return ev

    pb = 0
    assert NG1 % 2 == 0
    for pair in range(NG1 // 2):
        xn_ok = {}
        for gi in range(2):
            grp = 2 * pair + gi
            evs = []
            for j in range(GA1):
                blk = grp * GA1 + j
                evs.append(prep_block(xbt[blk], pb % 2, gi, j * 128,
                                      is_first=(pair == 0 and gi == 0 and j == 0)))
                pb += 1
            xn_ok[gi] = ("vEV", evs[-1])
        for o in range(KVOC):
            halves = load_slab()
            for gi in range(2):
                grp = 2 * pair + gi
                bank = (2 * o + gi) % 4
                mm = proj_ochunk(bank, lambda i, gi=gi: xnT2[:, gi, i, 0:512],
                                 halves, extra=[xn_ok[gi]], nw=512)
                par = gi
                if o < KVH:
                    p.op("scalar", lambda s, bank=bank, par=par: s.activation(
                        kstage[:, par, :], pp[:, bank, :], AF.Copy),
                        waits=[("mPE", mm), WAR.get(("kst", par), ("aR", 0))], inc="aEV")
                    bank_free[bank] = ("aEV", p.v("aEV"))
                    sv = sdma(f"dSK{par}", KDR[o][:, grp * 512:(grp + 1) * 512],
                              kstage[:, par, :], [("aEV", p.v("aEV"))])
                    WAR[("kst", par)] = sv
                else:
                    kv = o - KVH
                    p.op("scalar", lambda s, bank=bank, par=par: s.activation(
                        vstage[:, par, :], pp[:, bank, :], AF.Copy),
                        waits=[("mPE", mm), WAR.get(("vst", par), ("mTR", 0))], inc="aEV")
                    bank_free[bank] = ("aEV", p.v("aEV"))
                    aev = p.v("aEV")
                    ev = transpose_batch(
                        lambda j, par=par: vstage[:, par, 128 * j:128 * (j + 1)],
                        4, vtstage[:, par, :, :],
                        first_waits=[("aEV", aev)],
                        dst_war=[WAR.get(("vts", par), ("aR", 0))])
                    WAR[("vst", par)] = ("mTR", p.v("mTR"))
                    sv = sdma(f"dSV{par}",
                              VDR[kv, grp * GA1:(grp + 1) * GA1].rearrange("b t d -> t b d"),
                              vtstage[:, par, :, :], [("vEV", ev)])
                    WAR[("vts", par)] = sv
        WAR[("xnT", 0)] = ("mPE", p.v("mPE"))
        WAR[("xnT", 1)] = ("mPE", p.v("mPE"))

    A1_K = [("dSK0", p.v("dSK0")), ("dSK1", p.v("dSK1"))]
    A1_V = [("dSV0", p.v("dSV0")), ("dSV1", p.v("dSV1"))]

    for gi in range(2):
        for j in range(OWNG):
            prep_block(xow[gi * OWNG + j], pb % 2, gi, j * 128, is_first=False)
            pb += 1
    a2_ok = ("vEV", p.v("vEV"))

    for o in range(QOC):
        halves = load_slab()
        for gi in range(2):
            bank = (2 * o + gi) % 4
            mm = proj_ochunk(bank, lambda i, gi=gi: xnT2[:, gi, i, 0:NA2T],
                             halves, extra=[a2_ok], nw=NA2T)
            par = (2 * o + gi) % 2
            p.op("scalar", lambda s, bank=bank, par=par: s.activation(
                qstage[:, par, 0:NA2T], pp[:, bank, 0:NA2T], AF.Copy, scale=QSCALE),
                waits=[("mPE", mm), WAR.get(("qst", par), ("aR", 0))], inc="aEV")
            bank_free[bank] = ("aEV", p.v("aEV"))
            sv = sdma(f"dSQ{par}", QDR[o // G, o % G, :, gi * NA2T:(gi + 1) * NA2T],
                      qstage[:, par, 0:NA2T], [("aEV", p.v("aEV"))])
            WAR[("qst", par)] = sv

    A2_Q = [("dSQ0", p.v("dSQ0")), ("dSQ1", p.v("dSQ1"))]
    A_PE_DONE = p.v("mPE")
    A_TR_DONE = p.v("mTR")

    nQK = 0
    exp_vals = []
    it_vals = []
    kv_last_it = {}
    qsl_war = {}
    msl_war = {}
    pv_free = {0: ("vAT", 0), 1: ("vAT", 0)}
    ones_free = {0: ("vAT", 0), 1: ("vAT", 0)}
    lnt_war = {0: ("vAT", 0), 1: ("vAT", 0)}
    QKB = (0, 1, 4)
    PVB = {0: 2, 1: 5}
    ONB = {0: 3, 1: 6}

    def emit_qk(i, t, slot, qslot, krdy, qrdy):
        sbank = QKB[i % 3]
        waits = []
        if t == 0:
            waits += [krdy, qrdy]
        if i >= 3:
            waits.append(("aEX", exp_vals[i - 3]))
        p.op("tensor", lambda t_, sbank=sbank, slot=slot, qslot=qslot, tt=t: t_.matmul(
            pp[:, sbank, :], ksl[:, slot, tt, :],
            qsl[:, qslot].rearrange("p g t -> p (g t)"),
            start=True, stop=True, skip_group_check=True),
            waits=waits, inc="mQK")
        ewaits = [("mQK", p.v("mQK"))]
        if i >= 3:
            ewaits.append(("mIT", it_vals[i - 3]))
        p.op("scalar", lambda s, sbank=sbank, sl3=i % 3: s.activation(
            psl[:, sl3, :], pp[:, sbank, :], AF.Exp,
            bias=bsb[:, SHIFT_COL:SHIFT_COL + 1]),
            waits=ewaits, inc="aEX")
        exp_vals.append(p.v("aEX"))

    for kv in range(KVH):
        slot = kv % 2
        kwaits = list(A1_K)
        vwaits = list(A1_V)
        if kv >= 2:
            kwaits.append(("mIT", kv_last_it[kv - 2]))
        krdy = sdma(f"dLK{slot}", ksl[:, slot, :, :],
                    KDR[kv].rearrange("p (c t) -> p c t", c=NBLK), kwaits)
        vrdy = sdma(f"dLV{slot}", vsl[:, slot, :, :],
                    VDR[kv].rearrange("b t d -> t b d"), vwaits)
        for qb in range(NOWN):
            pad = PADS[qb]
            par = qb % 2
            qslot = qb % 2
            qwaits = list(A2_Q)
            if qslot in qsl_war:
                qwaits.append(("mQK", qsl_war[qslot]))
            qrdy = sdma(f"dLQ{qslot}", qsl[:, qslot, :, :],
                        QDR[kv, :, :, qb * 128:(qb + 1) * 128].rearrange("g d t -> d g t"),
                        qwaits)
            mwaits = [("vMS", msl_war[qslot])] if qslot in msl_war else []
            mrdy = sdma(f"dLM{qslot}", msl[:, qslot, :, :],
                        mtab[qb].rearrange("w p q -> p w q"), mwaits)

            base = nQK
            for t in range(min(3, pad)):
                emit_qk(base + t, t, slot, qslot, krdy, qrdy)
            for t in range(pad):
                i = base + t
                sl3 = i % 3
                pv_gate = ("aEX", exp_vals[i])
                if t >= pad - R:
                    w = t - (pad - R)
                    p.op("vector", lambda v, sl3=sl3, qslot=qslot, w=w: v.tensor_tensor(
                        psl[:, sl3, :], psl[:, sl3, :], msl[:, qslot, w, :], ALU.mult),
                        waits=[("aEX", exp_vals[i]), mrdy], inc="vMS")
                    pv_gate = ("vMS", p.v("vMS"))
                    msl_war[qslot] = p.v("vMS")
                pvw = [pv_gate]
                if t == 0:
                    pvw += [vrdy, pv_free[par]]
                p.op("tensor", lambda t_, slot=slot, sl3=sl3, tt=t, pb=PVB[par],
                     st=(t == 0), sp=(t == pad - 1): t_.matmul(
                    pp[:, pb, :], vsl[:, slot, tt, :], psl[:, sl3, :],
                    start=st, stop=sp, skip_group_check=True),
                    waits=pvw)
                ow = [ones_free[par]] if t == 0 else []
                p.op("tensor", lambda t_, sl3=sl3, ob=ONB[par],
                     st=(t == 0), sp=(t == pad - 1): t_.matmul(
                    pp[:, ob, :], onesb, psl[:, sl3, :],
                    start=st, stop=sp, skip_group_check=True),
                    waits=ow, inc="mIT")
                nQK += 1
                it_vals.append(p.v("mIT"))
                if t + 3 < pad:
                    emit_qk(base + t + 3, t + 3, slot, qslot, krdy, qrdy)
            qsl_war[qslot] = p.v("mQK")
            itv = p.v("mIT")
            p.op("vector", lambda v, par=par: v.reciprocal(lntmp[:, par, :], pp[:, ONB[par], :]),
                 waits=[("mIT", itv), lnt_war[par]], inc="vAT")
            rcv = p.v("vAT")
            ones_free[par] = ("vAT", rcv)
            awar = [("mTR", A_TR_DONE), ("vLN", p.v("vLN"))] if (kv, qb) == (0, 0) else []
            p.op("vector", lambda v, kv=kv, qb=qb, par=par: v.tensor_tensor(
                attn[:, G * kv:G * (kv + 1), qb * 128:(qb + 1) * 128],
                pp[:, PVB[par], :].rearrange("p (g t) -> p g t", g=G),
                lntmp[:, par, :].rearrange("p (g t) -> p g t", g=G), ALU.mult),
                waits=[("vAT", rcv)] + awar, inc="vAT")
            pv_free[par] = ("vAT", p.v("vAT"))
            lnt_war[par] = ("vAT", p.v("vAT"))
        kv_last_it[kv] = p.v("mIT")

    B_AT = p.v("vAT")
    B_IT = p.v("mIT")
    B_MS = p.v("vMS")
    B_EX = p.v("aEX")
    bank_free[0] = ("aEX", B_EX)
    bank_free[1] = ("aEX", B_EX)
    bank_free[4] = ("aEX", B_EX)
    bank_free[2] = ("vAT", B_AT)
    bank_free[3] = ("vAT", B_AT)
    bank_free[5] = ("vAT", B_AT)
    bank_free[6] = ("vAT", B_AT)

    for o in range(IC):
        halves = load_slab()
        for th in range(NTH):
            bank = (NTH * o + th) % 4
            extra = [("vAT", B_AT)] if (o == 0 and th == 0) else []
            mm = proj_ochunk(bank, lambda i, th=th: attn[:, i, th * THW:(th + 1) * THW],
                             halves, extra=extra, nw=THW)
            par = th % 2
            p.op("scalar", lambda s, bank=bank, par=par: s.activation(
                ydst[:, par, 0:THW], pp[:, bank, 0:THW], AF.Copy),
                waits=[("mPE", mm), WAR.get(("yd", par), ("mTR", 0))], inc="aEV")
            bank_free[bank] = ("aEV", p.v("aEV"))
            aev = p.v("aEV")
            ev = transpose_batch(
                lambda j, par=par: ydst[:, par, 128 * j:128 * (j + 1)],
                NB_TH, ytm[:, th * NB_TH:(th + 1) * NB_TH, o * 128:(o + 1) * 128],
                first_waits=[("aEV", aev)],
                dst_war=([("mPE", A_PE_DONE)] if (o == 0 and th == 0) else ()))
            WAR[("yd", par)] = ("mTR", p.v("mTR"))
    C_MM = p.v("mPE")
    C_EV = p.v("vEV")

    XN2T_TH = {}
    for tb in range(NOWN):
        par = tb % 2
        w0 = [("mPE", C_MM)] if tb < 2 else [WAR.get(("xstr", par), ("vLN", 0))]
        p.op("gpsimd", lambda g, tb=tb, par=par: g.dma_start(xstrip[:, par, :], xow[tb]),
             waits=w0, inc=f"dX{par}", incn=16)
        dx = (f"dX{par}", p.v(f"dX{par}"))
        ar = ln_stats(ytm[:, tb, :], par, first_waits=[("vEV", C_EV)])
        x2w = WAR.get(("x2s", par), ("aR", 0))
        p.op("vector", lambda v, tb=tb, par=par: v.tensor_scalar(
            x2stage[:, par, :], ytm[:, tb, :], mvt[:, par, 0:1], rst[:, par, 0:1],
            ALU.subtract, ALU.mult),
            waits=[("aR", ar), x2w] + ([("mTR", A_TR_DONE)] if tb < 2 else []), inc="vLN")
        vv = p.v("vLN")
        p.op("vector", lambda v, par=par: v.tensor_tensor(
            x2stage[:, par, :], x2stage[:, par, :], xstrip[:, par, :], ALU.add),
            waits=[("vLN", vv), dx], inc="vLN")
        vv2 = p.v("vLN")
        sv = sdma(f"dSX{par}", X2DR[tb], x2stage[:, par, :], [("vLN", vv2)])
        WAR[("x2s", par)] = sv
        ar3 = ln_stats(x2stage[:, par, :], par, first_waits=[("vLN", vv2)])
        nv = ln_norm(xstrip[:, par, :], x2stage[:, par, :], par, ar3)
        dst_war = ([("mIT", B_IT), ("vAT", B_AT), ("vMS", B_MS)] if tb == 0 else ())
        transpose_strip(
            xstrip[:, par, :], IC,
            lambda ic0, n, tb=tb: xn2T[:, ic0:ic0 + n, tb * 128:(tb + 1) * 128],
            src_ready=("vLN", nv), dst_war=dst_war)
        WAR[("xstr", par)] = ("mTR", p.v("mTR"))
        th_of = (tb + 1) * TB // THW - 1 if ((tb + 1) * TB) % THW == 0 else None
        if th_of is not None:
            XN2T_TH[th_of] = ("vEV", p.v("vEV"))
    XN2T_OK = ("vEV", p.v("vEV"))
    X2_ST = [("dSX0", p.v("dSX0")), ("dSX1", p.v("dSX1"))]

    for m in range(MC):
        h1 = load_slab()
        mm1 = {}
        mm2 = {}
        for th in range(NTH):
            extra = [XN2T_TH[th]] if m == 0 else []
            mm1[th] = proj_ochunk(th % 2, lambda i, th=th: xn2T[:, i, th * THW:(th + 1) * THW],
                                  h1, extra=extra, nw=THW)
        h2 = load_slab()
        for th in range(NTH):
            mm2[th] = proj_ochunk(2 + th % 2, lambda i, th=th: xn2T[:, i, th * THW:(th + 1) * THW],
                                  h2, nw=THW)
        for th in range(NTH):
            par = th % 2
            p.op("scalar", lambda s, th=th, par=par, m=m: s.activation(
                silst[:, par, 0:THW], pp[:, th % 2, 0:THW], AF.Silu, bias=bsb[:, m:m + 1]),
                waits=[("mPE", mm1[th]), WAR.get(("sil", par), ("vLN", 0))], inc="aEV")
            bank_free[th % 2] = ("aEV", p.v("aEV"))
            aev = p.v("aEV")
            x4w = ([("mPE", C_MM), ("mTR", p.v("mTR"))] + X2_ST) if (m == 0 and th == 0) else []
            p.op("vector", lambda v, th=th, par=par, m=m: v.scalar_tensor_tensor(
                x4[:, m, th * THW:(th + 1) * THW], pp[:, 2 + th % 2, 0:THW], bsb[:, MC + m:MC + m + 1],
                silst[:, par, 0:THW], ALU.add, ALU.mult),
                waits=[("mPE", mm2[th]), ("aEV", aev)] + x4w, inc="vLN")
            bank_free[2 + th % 2] = ("vLN", p.v("vLN"))
            WAR[("sil", par)] = ("vLN", p.v("vLN"))
    X4_OK = ("vLN", p.v("vLN"))

    for o in range(IC):
        halves = load_slab()
        for th in range(NTH):
            bank = 4 + th % 2
            extra = [X4_OK] if (o == 0 and th == 0) else []
            mm = proj_ochunk(bank, lambda i, th=th: x4[:, i, th * THW:(th + 1) * THW],
                             halves, extra=extra, nw=THW)
            par = th % 2
            p.op("vector", lambda v, bank=bank, par=par, o=o: v.tensor_scalar_add(
                ydst[:, par, 0:THW], pp[:, bank, 0:THW],
                bsb[:, 2 * MC + o:2 * MC + o + 1]),
                waits=[("mPE", mm), WAR.get(("yd", par), ("mTR", 0))], inc="vEV")
            bank_free[bank] = ("vEV", p.v("vEV"))
            aev = p.v("vEV")
            transpose_batch(
                lambda j, par=par: ydst[:, par, 128 * j:128 * (j + 1)],
                NB_TH, ytm[:, th * NB_TH:(th + 1) * NB_TH, o * 128:(o + 1) * 128],
                first_waits=[("vEV", aev)])
            WAR[("yd", par)] = ("mTR", p.v("mTR"))
    E_MM = p.v("mPE")
    E_EV = p.v("vEV")

    for tb in range(NOWN):
        par = tb % 2
        w0 = ([("mPE", E_MM)] + X2_ST) if tb < 2 else [WAR.get(("xse", par), ("vLN", 0))]
        xrdy = sdma(f"dLX{par}", xstrip[:, par, :], X2DR[tb], w0)
        ar = ln_stats(ytm[:, tb, :], par, first_waits=[("vEV", E_EV)])
        p.op("vector", lambda v, tb=tb, par=par: v.tensor_scalar(
            x2stage[:, par, :], ytm[:, tb, :], mvt[:, par, 0:1], rst[:, par, 0:1],
            ALU.subtract, ALU.mult),
            waits=[("aR", ar), WAR.get(("x2e", par), ("vLN", 0))], inc="vLN")
        vv = p.v("vLN")
        p.op("vector", lambda v, par=par: v.tensor_tensor(
            ostage[:, par, :], x2stage[:, par, :], xstrip[:, par, :], ALU.add),
            waits=[("vLN", vv), xrdy, WAR.get(("ost", par), ("aR", 0))],
            inc="vLN")
        vv2 = p.v("vLN")
        WAR[("x2e", par)] = ("vLN", vv2)
        WAR[("xse", par)] = ("vLN", vv2)
        p.op("sync", lambda sy, tb=tb, par=par: sy.dma_start(yout[tb], ostage[:, par, :]),
             waits=[("vLN", vv2)], inc=f"dO{par}", incn=16)
        WAR[("ost", par)] = (f"dO{par}", p.v(f"dO{par}"))

    p.op("sync", lambda sy: sy.nop(),
         waits=[("dO0", p.v("dO0")), ("dO1", p.v("dO1"))])

    assert half_loaded[0] == 2 * NSLAB, (half_loaded[0], NSLAB)
    p.audit()

    with nc.Block() as block:
        @block.tensor
        def _(t):
            p.replay("tensor", t)

        @block.scalar
        def _(s):
            p.replay("scalar", s)

        @block.vector
        def _(v):
            p.replay("vector", v)

        @block.gpsimd
        def _(g):
            p.replay("gpsimd", g)

        @block.sync
        def _(sy):
            p.replay("sync", sy)

    ctx.close()
    return nc



def make_tape(cfg, w_qkv, w_dense, w12, w3):
    d = derived(cfg)
    IC, KVH, MC = cfg["IC"], cfg["KVH"], cfg["MC"]
    D, M = d["D"], d["M"]

    def slabs(W):
        O = W.shape[0]
        W4 = W.reshape(O // 128, 128, IC, 128)
        return np.ascontiguousarray(W4.transpose(0, 3, 2, 1))

    wq = w_qkv[0:D]
    wk = w_qkv[D:D + KVH * 128]
    wv = w_qkv[D + KVH * 128:D + 2 * KVH * 128]
    kv_slabs = np.concatenate([slabs(wk), slabs(wv)], axis=0)
    s1 = slabs(w12[0:M])
    s2 = slabs(w12[M:2 * M])
    inter = np.empty((2 * MC, 128, IC, 128), dtype=w12.dtype)
    inter[0::2] = s1
    inter[1::2] = s2
    tape = np.concatenate([np.concatenate([kv_slabs] * (d["NG1"] // 2), axis=0),
                           slabs(wq), slabs(w_dense), inter, slabs(w3)], axis=0)
    assert tape.shape[0] == d["NSLAB"]
    return np.ascontiguousarray(tape)


def make_mtab(cfg, s):
    d = derived(cfg)
    NOWN = cfg["NOWN"]
    blocks = own_blocks(s, cfg)
    mt = np.zeros((NOWN, R, 128, 512), dtype=BF)
    tri = np.tril(np.ones((128, 128), dtype=np.float32))
    keep = np.tile(tri.T, (1, 4)).astype(BF)
    one = np.ones((128, 512), dtype=BF)
    for qb in range(NOWN):
        pad = d["PADS"][qb]
        jb = blocks[qb]
        for w in range(R):
            t_iter = pad - R + w
            if t_iter < jb:
                mt[qb, w] = one
            elif t_iter == jb:
                mt[qb, w] = keep
    return mt


def stage_inputs(cfg, x, w_qkv, w_dense, w12, b12, w3, b3):
    d = derived(cfg)
    NBLK, IC, MC = cfg["NBLK"], cfg["IC"], cfg["MC"]
    D, M = d["D"], d["M"]
    tape = make_tape(cfg, w_qkv, w_dense, w12, w3)
    idon = np.zeros((2, 128, 128), dtype=BF)
    idon[0] = np.eye(128, dtype=np.float32).astype(BF)
    idon[1] = 1.0
    bcst = np.zeros((128, 2 * MC + IC + 2), dtype=np.float32)
    bcst[:, 0:MC] = b12[0:M].reshape(MC, 128).T
    bcst[:, MC:2 * MC] = b12[M:2 * M].reshape(MC, 128).T
    bcst[:, 2 * MC:2 * MC + IC] = b3.reshape(IC, 128).T
    bcst[:, 2 * MC + IC] = 1e-5
    bcst[:, 2 * MC + IC + 1] = EXP_SHIFT
    in_maps = []
    for c in range(8):
        b, s = c // 4, c % 4
        blocks = own_blocks(s, cfg)
        xb_ = np.ascontiguousarray(x[b].reshape(NBLK, TB, D))
        xo_ = np.ascontiguousarray(xb_[blocks])
        in_maps.append(dict(xb=xb_, xown=xo_, wtape=tape,
                            mtab=make_mtab(cfg, s), idon=idon, bcst=bcst))
    return in_maps


def assemble(cfg, results):
    d = derived(cfg)
    out = np.zeros((2, d["S"], d["D"]), dtype=np.float32)
    for c in range(8):
        b, s = c // 4, c % 4
        blocks = own_blocks(s, cfg)
        y = np.asarray(results[c]["yout"])
        for i, blk in enumerate(blocks):
            out[b, blk * TB:(blk + 1) * TB] = y[i]
    return out


def kernel(L0_attn_q, L0_attn_w_qkv, L0_attn_w_dense, L0_ffn_w12, L0_ffn_b12,
           L0_ffn_w3, L0_ffn_b3):
    cfg = full_cfg()
    x = np.ascontiguousarray(np.asarray(L0_attn_q))
    in_maps = stage_inputs(cfg, x, np.asarray(L0_attn_w_qkv), np.asarray(L0_attn_w_dense),
                           np.asarray(L0_ffn_w12), np.asarray(L0_ffn_b12),
                           np.asarray(L0_ffn_w3), np.asarray(L0_ffn_b3))
    nc = build(cfg)
    res = run_bass_kernel_spmd(nc, in_maps, core_ids=list(range(8)))
    return assemble(cfg, res.results)


if __name__ == "__main__":
    cfg = tiny_cfg() if (len(sys.argv) > 1 and sys.argv[1] == "tiny") else full_cfg()
    nc = build(cfg)
    print("built ok")
